# version 03
# baseline (speedup 1.0000x reference)
import os
import numpy as np
from contextlib import ExitStack

import concourse.bass as bass
import concourse.mybir as mybir
from concourse.bass_utils import run_bass_kernel_spmd

F32 = mybir.dt.float32
BF16 = mybir.dt.bfloat16
I32 = mybir.dt.int32
AF = mybir.ActivationFunctionType
ALU = mybir.AluOpType
AX = mybir.AxisListType

D = 1024
NT = 4096
C = 64
M = 33
KL = 4
LH = 29
HD = 64
NH = 16
NKV = 4
PAIR = 1024
NP = NT // PAIR
NEG = -1.7e38


def host_constants():
    t = np.arange(C)[:, None].astype(np.float64)
    mm_ = np.arange(M)[None, :].astype(np.float64)
    ang = 2.0 * np.pi * t * mm_ / C
    Fc = np.cos(ang)
    Fs = -np.sin(ang)
    w = np.full(M, 2.0); w[0] = 1.0; w[-1] = 1.0
    Gr = (w[None, :] * np.cos(ang) / C)
    Gi = (-w[None, :] * np.sin(ang) / C)
    F2 = np.zeros((128, 132))
    F2[0:64, 0:33] = Fc;   F2[0:64, 33:66] = Fs
    F2[64:128, 66:99] = Fc; F2[64:128, 99:132] = Fs
    F2l = np.zeros((128, 16))
    F2l[0:64, 0:4] = Fc[:, :4];   F2l[0:64, 4:8] = Fs[:, :4]
    F2l[64:128, 8:12] = Fc[:, :4]; F2l[64:128, 12:16] = Fs[:, :4]
    G2r = np.zeros((66, 128)); G2i = np.zeros((66, 128))
    G2r[0:33, 0:64] = Gr.T;  G2r[33:66, 64:128] = Gr.T
    G2i[0:33, 0:64] = Gi.T;  G2i[33:66, 64:128] = Gi.T
    tri = (np.arange(128)[:, None] < np.arange(128)[None, :]).astype(np.float32)
    ones128 = np.ones((128, 128), np.float32)
    eidx = np.broadcast_to(np.arange(16, dtype=np.float32), (128, 16)).copy()
    iota = np.arange(128, dtype=np.float32).reshape(128, 1)
    return (F2.astype(np.float32), F2l.astype(np.float32),
            G2r.astype(np.float32), G2i.astype(np.float32),
            np.eye(128, dtype=np.float32), tri, ones128, eidx, iota)


def host_masks(h):
    import ml_dtypes
    out = np.zeros((NP, 128, 512), np.float32)
    for p in range(NP):
        for rq in range(64):
            own_ch = 16 * p + rq // 4
            for ck in range(128):
                ok = (h == 1) if ck < 64 else ((ck - 64) <= own_ch)
                if not ok:
                    out[p, rq, 4 * ck:4 * ck + 4] = NEG
                    out[p, 64 + rq, 4 * ck:4 * ck + 4] = NEG
    return out.astype(ml_dtypes.bfloat16)



ENGS = ("te", "dve", "act", "pool", "sp")


def _region(ap, whole):
    tname = ap.tensor.name
    dims = list(ap.ap)
    if whole or not dims:
        return (tname, 0, 1 << 30, 0, 1 << 30)
    pstep, pcount = dims[0][0], dims[0][1]
    off = ap.offset
    if pstep and pstep > 0:
        pbase = off // pstep
        foff = off - pbase * pstep
    else:
        pbase, foff = 0, off
    span = 0
    for d in dims[1:]:
        span += abs(d[0]) * (d[1] - 1)
    return (tname, pbase, pbase + max(1, pcount), foff, foff + span + 1)


class Prog:
    def __init__(self, psum_name="ps"):
        self.nodes = []
        self.stream = {e: [] for e in ENGS}
        self.regions = {}
        self.n_dma_q = {}
        self.psum_name = psum_name

    def _touch(self, nid, items, write):
        deps = self.nodes[nid]["deps"]

        def add_dep(d, kind):
            if d is None or d == nid:
                return
            old = deps.get(d)
            if old is None or (old != "raw" and kind == "raw") or                (old == "war" and kind == "waw"):
                deps[d] = kind
        for ap, whole in items:
            tname, plo, phi, flo, fhi = _region(ap, whole)
            if tname == self.psum_name:
                plo, phi = 0, 128
                flo, fhi = (flo // 512) * 512, ((fhi - 1) // 512 + 1) * 512
            lst = self.regions.setdefault(tname, [])
            keep = []
            for rec in lst:
                if plo < rec[1] and rec[0] < phi and flo < rec[3] and rec[2] < fhi:
                    if write:
                        add_dep(rec[4], "waw")
                        for rd in rec[5]:
                            add_dep(rd, "war")
                        if rec[0] >= plo and rec[1] <= phi and rec[2] >= flo and rec[3] <= fhi:
                            continue
                    else:
                        add_dep(rec[4], "raw")
                        rec[5].add(nid)
                keep.append(rec)
            if write:
                keep.append([plo, phi, flo, fhi, nid, set()])
            self.regions[tname] = keep

    def op(self, eng, fn, reads=(), writes=(), reads_whole=(), writes_whole=(), fence=False):
        nid = len(self.nodes)
        self.nodes.append(dict(eng=eng, fn=fn, deps={}, dma=False, fence=fence))
        self._touch(nid, [(a, False) for a in reads] + [(a, True) for a in reads_whole], False)
        self._touch(nid, [(a, False) for a in writes] + [(a, True) for a in writes_whole], True)
        self.stream[eng].append(nid)
        return nid

    def flush(self, queue, writes_whole):
        n = self.n_dma_q.get(queue, 0)
        self._flush_req = (queue, 16 * n)

        def fn(h, _q=queue, _n=n, _p=self):
            for _s, _v in _p._flush_targets(_q, _n):
                h.wait_ge(_s, _v)
            return h.engine_nop()
        eng = "pool" if queue == "pool" else "sp"
        return self.op(eng, fn, writes_whole=writes_whole)

    def dma(self, out_ap, in_ap, out_whole=False, in_whole=False, queue="sp",
            fn=None, extra_reads=()):
        if fn is None:
            def fn(h, _o=out_ap, _i=in_ap):
                return h.dma_start(_o, _i)
        nid = self.op(queue, fn,
                      reads=(() if in_whole else (in_ap,)) + tuple(extra_reads),
                      writes=() if out_whole else (out_ap,),
                      reads_whole=(in_ap,) if in_whole else (),
                      writes_whole=(out_ap,) if out_whole else ())
        self.n_dma_q.setdefault(queue, 0)
        self.n_dma_q[queue] += 1
        self.nodes[nid].update(dma=True, dma_q=queue, dma_idx=self.n_dma_q[queue])
        return nid

    def finalize(self, nc, blk, ctx):
        nodes = self.nodes
        pos = {}
        for e in ENGS:
            for i, nid in enumerate(self.stream[e]):
                pos[nid] = i

        def needs_edge(dep, kind, nd):
            dn = nodes[dep]
            if dn["dma"] and nd["dma"]:
                return dn["dma_q"] != nd["dma_q"]
            if dn["dma"]:
                return True
            if dn["eng"] != nd["eng"]:
                return True
            if nd["dma"]:
                return False
            if kind == "war":
                return False
            if dn.get("fence"):
                return True
            return (pos[nd["nid"]] - pos[dep]) < 4

        for i, nd in enumerate(nodes):
            nd["nid"] = i
        tagged = set()
        pairs = set()
        for nd in nodes:
            for dep, kind in nd["deps"].items():
                dn = nodes[dep]
                if dn["dma"]:
                    continue
                if not needs_edge(dep, kind, nd):
                    continue
                tagged.add(dep)
                pairs.add((dn["eng"], nd["eng"]))
        srcs = sorted({pr[0] for pr in pairs})
        sems = {e: ctx.enter_context(nc.semaphore("s_" + e)) for e in srcs}
        QK_ = {"sp": 8, "pool": 4}
        dma_sems = {}
        for q in sorted(self.n_dma_q):
            k = QK_.get(q, 4)
            dma_sems[q] = [ctx.enter_context(nc.semaphore("s_dma_%s%d" % (q, i)))
                           for i in range(k)]
        def dma_sem_and_count(q, idx):
            k = len(dma_sems[q])
            return dma_sems[q][(idx - 1) % k], 16 * ((idx - 1) // k + 1)

        def _flush_targets(q, n):
            k = len(dma_sems[q])
            out = []
            for i in range(k):
                cnt = (n - 1 - i) // k + 1 if n > i else 0
                if cnt > 0:
                    out.append((dma_sems[q][i], 16 * cnt))
            return out
        self._flush_targets = _flush_targets
        tagcount = {}
        for e in ENGS:
            cnt = 0
            for nid in self.stream[e]:
                if nid in tagged:
                    cnt += 1
                tagcount[nid] = cnt
        for nd in nodes:
            waits = {}
            for dep, kind in nd["deps"].items():
                dn = nodes[dep]
                if not needs_edge(dep, kind, nd):
                    continue
                if dn["dma"]:
                    kq = len(dma_sems[dn["dma_q"]])
                    sem_i = (dn["dma_idx"] - 1) % kq
                    key = ("dma", dn["dma_q"], sem_i)
                    v = 16 * ((dn["dma_idx"] - 1) // kq + 1)
                else:
                    key, v = dn["eng"], tagcount[dep]
                if waits.get(key, 0) < v:
                    waits[key] = v
            nd["waits"] = waits

        def emit(eng, h):
            sat = {}
            my_sem = sems.get(eng)
            for nid in self.stream[eng]:
                nd = nodes[nid]
                for key in sorted(nd["waits"], key=str):
                    v = nd["waits"][key]
                    if sat.get(key, 0) >= v:
                        continue
                    sem = dma_sems[key[1]][key[2]] if isinstance(key, tuple) else sems[key]
                    h.wait_ge(sem, v)
                    sat[key] = v
                if nd["dma"]:
                    k = len(dma_sems[nd["dma_q"]])
                    si = (nd["dma_idx"] - 1) % k
                    cnt = (nd["dma_idx"] - 1) // k + 1
                    if cnt > 1:
                        h.wait_ge(dma_sems[nd["dma_q"]][si], 16 * (cnt - 1))
                    inst = nd["fn"](h)
                    inst.then_inc(dma_sems[nd["dma_q"]][si], 16)
                    continue
                inst = nd["fn"](h)
                if nid in tagged:
                    inst.then_inc(my_sem, 1)

        @blk.tensor
        def _(h):
            emit("te", h)

        @blk.vector
        def _(h):
            emit("dve", h)

        @blk.scalar
        def _(h):
            emit("act", h)

        @blk.gpsimd
        def _(h):
            emit("pool", h)

        @blk.sync
        def _(h):
            emit("sp", h)
            for q in sorted(self.n_dma_q):
                for sem_, v_ in _flush_targets(q, self.n_dma_q[q]):
                    h.wait_ge(sem_, v_)



def build_program(n_pairs=NP, prologue=True, with_attn=True, with_high=True,
                  with_ffn=True, n_experts=16, taps=()):
    nc = bass.Bass("TRN2", target_bir_lowering=False)
    din = lambda name, shape, dt=F32: nc.dram_tensor(name, shape, dt, kind="ExternalInput").ap()
    x_own = din("x_own", [NT, D])
    x_prev = din("x_prev", [NT, D])
    qwT = din("qwT", [D, D])
    kvwT = din("kvwT", [D, 512])
    owT = din("owT", [D, D])
    gwT = din("gwT", [D, D])
    pwrT = din("pwrT", [D, D])
    pwiT = din("pwiT", [D, D])
    dwp = din("dwp", [128, 8, 2, 7])
    alpha_in = din("alpha_in", [128, 1])
    SgD = din("SgD", [D, D]); SuD = din("SuD", [D, D]); SdD = din("SdD", [D, D])
    Wrp = din("Wrp", [128, 8, 16])
    WgD = din("WgD", [16, D, 512]); WuD = din("WuD", [16, D, 512]); WdD = din("WdD", [16, 512, D])
    cF2 = din("cF2", [128, 132]); cF2l = din("cF2l", [128, 16])
    cG2r = din("cG2r", [66, 128]); cG2i = din("cG2i", [66, 128])
    cId = din("cId", [128, 128])
    cMask = din("cMask", [NP, 128, 512], BF16)
    cTri = din("cTri", [128, 128]); cOnes = din("cOnes", [128, 128])
    cEidx = din("cEidx", [128, 16]); cIota = din("cIota", [128, 1])
    Xg_d = nc.dram_tensor("Xg_d", [16 * 256 + 128, 1026], F32, kind="Internal").ap()
    routA_d = nc.dram_tensor("routA_d", [1152, 1024], F32, kind="Internal").ap()
    routB_d = nc.dram_tensor("routB_d", [1152, 1024], F32, kind="Internal").ap()
    routSH_d = nc.dram_tensor("routSH_d", [1024, 1024], F32, kind="Internal").ap()
    out_d = nc.dram_tensor("out", [NT, D], F32, kind="ExternalOutput").ap()
    tap_d = {}
    for tname, shape in taps:
        tap_d[tname] = nc.dram_tensor("tap_" + tname, list(shape), F32,
                                      kind="ExternalOutput").ap()

    ctx = ExitStack()
    sb = lambda name, shape, dt=F32: ctx.enter_context(nc.sbuf_tensor(name, shape, dt))
    xp = sb("xp", [128, 8, 1024])
    xn1 = sb("xn1", [128, 2, 1024])
    xlow = sb("xlow", [128, 8, 2, 16, 4])
    QK = sb("QK", [64, 6144])
    Vt = sb("Vt", [128, 2, 4, 256])
    yhat = sb("yhat", [128, 8, 2, 16, 33])
    g_sb = sb("g_sb", [128, 8, 16])
    SCR = sb("SCR", [128, 20608])
    ring = sb("ring", [128, 5, 512])
    F2s = sb("F2s", [128, 132]); F2ls = sb("F2ls", [128, 16])
    G2rs = sb("G2rs", [66, 128]); G2is = sb("G2is", [66, 128])
    idents = sb("idents", [128, 128])
    masks = sb("masks", [128, 512], BF16)
    dws = sb("dws", [128, 8, 2, 7])
    wrs = sb("wrs", [128, 8, 16])
    alph = sb("alph", [128, 1]); alph1m = sb("alph1m", [128, 1])
    stat = sb("stat", [128, 12])
    mean_t = sb("mean_t", [128, 1]); var_t = sb("var_t", [128, 1]); rstd_t = sb("rstd_t", [128, 1])
    m4 = sb("m4", [128, 4]); sm_t = sb("sm_t", [128, 1]); sq_t = sb("sq_t", [128, 1])
    nmx = sb("nmx", [128, 1]); nmx8 = sb("nmx8", [128, 1])
    ssum = sb("ssum", [128, 1]); rsum = sb("rsum", [128, 1])
    probs = sb("probs", [128, 16]); pmx = sb("pmx", [128, 1]); pm2 = sb("pm2", [128, 16])
    msk1 = sb("msk1", [128, 16]); msk2 = sb("msk2", [128, 16])
    ones_c = sb("ones_c", [128, 1]); nbig_c = sb("nbig_c", [128, 1]); eps_c = sb("eps_c", [128, 1])
    tiny = sb("tiny", [128, 4])
    tri_s = sb("tri_s", [128, 128]); ones_s = sb("ones_s", [128, 128])
    eidx_s = sb("eidx_s", [128, 16]); iota_s = sb("iota_s", [128, 1])
    indA_s = sb("indA_s", [128, 8, 16]); indB_s = sb("indB_s", [128, 8, 16])
    idxA = sb("idxA", [128, 1], I32); idxB = sb("idxB", [128, 1], I32)
    tokI = sb("tokI", [128, 1], I32)
    sA = sb("sA", [128, 1]); sB = sb("sB", [128, 1])
    zz2 = sb("zz2", [128, 64])
    ps = ctx.enter_context(nc.psum_tensor("ps", [128, 4096], F32))
    psap = ps.ap()
    if os.environ.get("BK_DEBUG"):
        print("sbuf bytes remaining:", nc.sbuf_bytes_remaining)

    P = Prog("ps")

    SCR_N = 20608
    def scrap(off, dims):
        apl = [[SCR_N, 128]]
        strides, tot = [], 1
        for d in reversed(dims):
            strides.insert(0, tot)
            tot *= d
        for d, s in zip(dims, strides):
            apl.append([s, d])
        return bass.AP(SCR, off, apl)

    def scr_ap(off, apl_free):
        return bass.AP(SCR, off, [[SCR_N, 128]] + [list(a) for a in apl_free])

    OFF_XN2T = 0
    OFF_STA = 8192
    OFF_STB = 10244
    OFF_XG = 12296
    OFF_YST = 14348
    OFF_HS = 16384
    OFF_HE = 16384
    OFF_CW = 20480
    OFF_CONV = 0
    OFF_SSB = 1856
    OFF_PSB = 2368
    OFF_TMP = 2880
    OFF_PT = 3200
    OFF_OCAT = 3712
    OFF_YBM = 4736

    def mm(out, lhsT, rhs, start, stop, tp=None):
        def fn(h, _o=out, _l=lhsT, _r=rhs, _s=start, _p=stop, _t=tp):
            return h.matmul(_o, _l, _r, start=_s, stop=_p, tile_position=_t)
        P.op("te", fn, reads=(lhsT, rhs), writes=(out,))

    def transpose(out, in_):
        def fn(h, _o=out, _i=in_):
            return h.transpose(_o, _i, idents.ap())
        P.op("te", fn, reads=(in_, idents.ap()), writes=(out,))

    def cp(eng, out, in_):
        def fn(h, _o=out, _i=in_):
            return h.tensor_copy(_o, _i)
        P.op(eng, fn, reads=(in_,), writes=(out,))

    def tt(eng, out, a, b, op):
        def fn(h, _o=out, _a=a, _b=b, _op=op):
            return h.tensor_tensor(_o, _a, _b, _op)
        P.op(eng, fn, reads=(a, b), writes=(out,))

    def tsc(eng, out, in_, s1, op0, s2=None, op1=None):
        rd = [in_] + [s for s in (s1, s2) if isinstance(s, bass.AP)]
        def fn(h, _o=out, _i=in_, _1=s1, _2=s2, _a=op0, _b=op1):
            if _b is None:
                return h.tensor_scalar(_o, _i, _1, None, _a)
            return h.tensor_scalar(_o, _i, _1, _2, _a, _b)
        P.op(eng, fn, reads=tuple(rd), writes=(out,))

    def stt(eng, out, in0, scalar, in1, op0, op1):
        rd = [in0, in1] + ([scalar] if isinstance(scalar, bass.AP) else [])
        def fn(h, _o=out, _a=in0, _s=scalar, _b=in1, _0=op0, _1=op1):
            return h.scalar_tensor_tensor(_o, _a, _s, _b, _0, _1)
        P.op(eng, fn, reads=tuple(rd), writes=(out,))

    def act(out, in_, func, bias=0.0, scale=1.0, accum=None):
        rd = [in_] + ([bias] if isinstance(bias, bass.AP) else [])
        wr = [out] + ([accum] if accum is not None else [])
        def fn(h, _o=out, _i=in_, _f=func, _b=bias, _s=scale, _a=accum):
            if _a is None:
                return h.activation(_o, _i, _f, bias=_b, scale=_s)
            return h.activation(_o, _i, _f, bias=_b, scale=_s, accum_out=_a)
        P.op("act", fn, reads=tuple(rd), writes=tuple(wr))

    def red(eng, out, in_, op, negate=False):
        def fn(h, _o=out, _i=in_, _op=op, _n=negate):
            return h.tensor_reduce(_o, _i, axis=AX.X, op=_op, negate=_n)
        P.op(eng, fn, reads=(in_,), writes=(out,))

    def bnstats(out, in_):
        def fn(h, _o=out, _i=in_):
            return h.bn_stats(_o, _i)
        P.op("dve", fn, reads=(in_,), writes=(out,), fence=True)

    def recip(out, in_):
        def fn(h, _o=out, _i=in_):
            return h.reciprocal(_o, _i)
        P.op("dve", fn, reads=(in_,), writes=(out,), fence=True)

    def memset(eng, ap_, val):
        def fn(h, _a=ap_, _v=val):
            return h.memset(_a, _v)
        P.op(eng, fn, writes=(ap_,))

    def tap(name, src_ap):
        if name in tap_d:
            P.dma(tap_d[name], src_ap, out_whole=True)

    ring_next = [0]
    RU = 5
    def ring_load(src_ap, units, w=None):
        u0 = ring_next[0]
        if units == 2 and (u0 % RU) == RU - 1:
            u0 += 1
        slot = u0 % RU
        ring_next[0] = u0 + units
        w = w if w is not None else units * 512
        dst = bass.AP(ring, 512 * slot, [[RU * 512, 128], [1, w]])
        P.dma(dst, src_ap, in_whole=True)
        return dst

    def layer_norm(dst_row, src_row):
        bnstats(stat.ap()[:, 0:6], src_row[:, 0:512])
        bnstats(stat.ap()[:, 6:12], src_row[:, 512:1024])
        mm4 = bass.AP(stat, 1, [[12, 128], [3, 4]])
        cv4 = bass.AP(stat, 2, [[12, 128], [3, 4]])
        red("dve", mean_t.ap(), mm4, ALU.add)
        tsc("dve", mean_t.ap(), mean_t.ap(), 0.25, ALU.mult)
        tt("dve", m4.ap(), mm4, mm4, ALU.mult)
        red("dve", sq_t.ap(), m4.ap(), ALU.add)
        red("dve", sm_t.ap(), cv4, ALU.add)
        tsc("dve", sq_t.ap(), sq_t.ap(), 0.25, ALU.mult)
        tsc("dve", sm_t.ap(), sm_t.ap(), 1.0 / 1024.0, ALU.mult)
        tt("dve", var_t.ap(), sm_t.ap(), sq_t.ap(), ALU.add)
        tt("dve", tiny.ap()[:, 0:1], mean_t.ap(), mean_t.ap(), ALU.mult)
        tt("dve", var_t.ap(), var_t.ap(), tiny.ap()[:, 0:1], ALU.subtract)
        act(var_t.ap(), var_t.ap(), AF.Ln, bias=eps_c.ap())
        act(rstd_t.ap(), var_t.ap(), AF.Exp, scale=-0.5)
        tsc("dve", dst_row, src_row, mean_t.ap(), ALU.subtract, rstd_t.ap(), ALU.mult)

    def q2_ap(hp, ri):
        return bass.AP(QK, hp * 256 + ri * 128, [[6144, 64], [1, 128]])

    def k_ap(kv, ri, n):
        return bass.AP(QK, 2048 + (kv * 2 + ri) * 512, [[6144, 64], [1, n]])

    P.dma(F2s.ap(), cF2, in_whole=True)
    P.dma(F2ls.ap(), cF2l, in_whole=True)
    P.dma(G2rs.ap(), cG2r, in_whole=True)
    P.dma(G2is.ap(), cG2i, in_whole=True)
    P.dma(idents.ap(), cId, in_whole=True)
    P.dma(dws.ap(), dwp, in_whole=True)
    P.dma(wrs.ap(), Wrp, in_whole=True)
    P.dma(alph.ap(), alpha_in, in_whole=True)
    P.dma(tri_s.ap(), cTri, in_whole=True)
    P.dma(ones_s.ap(), cOnes, in_whole=True)
    P.dma(eidx_s.ap(), cEidx, in_whole=True)
    P.dma(iota_s.ap(), cIota, in_whole=True)
    memset("dve", bass.AP(zz2, 0, [[64, 128], [2, 32]]), 0.0)
    memset("dve", bass.AP(zz2, 1, [[64, 128], [2, 32]]), 1024.0)
    memset("dve", ones_c.ap(), 1.0)
    memset("dve", nbig_c.ap(), -1e30)
    memset("dve", eps_c.ap(), 1e-5)
    memset("dve", SCR.ap()[:, 0:1026], 0.0)
    for blk_i in range(33):
        P.dma(Xg_d[128 * blk_i:128 * blk_i + 128, :], SCR.ap()[:, 0:1026], out_whole=True)
    act(tiny.ap()[:, 0:1], alph.ap(), AF.Exp, scale=-1.0)
    tt("dve", tiny.ap()[:, 0:1], tiny.ap()[:, 0:1], ones_c.ap(), ALU.add)
    recip(alph.ap(), tiny.ap()[:, 0:1])
    tt("dve", alph1m.ap(), ones_c.ap(), alph.ap(), ALU.subtract)

    def load_x(pidx, src):
        P.dma(xp.ap(),
              src[PAIR * pidx: PAIR * (pidx + 1), :].rearrange("(r q) d -> q r d", q=128),
              in_whole=True)

    def ln_dft(r, low_only):
        row = xn1.ap()[:, r % 2, :]
        layer_norm(row, xp.ap()[:, r, :])
        if low_only:
            for ft in range(8):
                mm(psap[:, 16 * ft:16 * ft + 16], row[:, 128 * ft:128 * ft + 128],
                   F2ls.ap(), start=True, stop=True)
            src = bass.AP(ps, 0, [[4096, 128], [16, 8], [8, 2], [4, 2], [1, 4]])
            dst = bass.AP(xlow, 8 * r, [[1024, 128], [128, 8], [4, 2], [64, 2], [1, 4]])
            cp("dve", dst, src)
        else:
            banks = [(0, 0), (0, 132), (0, 264), (1, 0), (1, 132), (1, 264), (2, 0), (2, 132)]
            for ft in range(8):
                b, c0 = banks[ft]
                mm(psap[:, 512 * b + c0: 512 * b + c0 + 132],
                   row[:, 128 * ft:128 * ft + 128], F2s.ap(), start=True, stop=True)
            for b, ft0, nft in ((0, 0, 3), (1, 3, 3), (2, 6, 2)):
                src_lo = bass.AP(ps, 512 * b, [[4096, 128], [132, nft], [66, 2], [33, 2], [1, 4]])
                dst_lo = bass.AP(xlow, 128 * ft0 + 8 * r,
                                 [[1024, 128], [128, nft], [4, 2], [64, 2], [1, 4]])
                cp("dve", dst_lo, src_lo)
                src_hi = bass.AP(ps, 512 * b + 4, [[4096, 128], [132, nft], [66, 2], [33, 2], [1, 29]])
                dst_hi = bass.AP(yhat, 1056 * ft0 + 66 * r + 4,
                                 [[8448, 128], [1056, nft], [33, 2], [528, 2], [1, 29]])
                cp("dve", dst_hi, src_hi)

    def kv_pass(slot0):
        kb, rh = slot0 // 128, (slot0 % 128) // 64
        for c in range(8):
            piece = ring_load(kvwT[128 * c:128 * c + 128, :], 1)
            for kv in range(NKV):
                mm(psap[0:64, 512 * kv:512 * kv + 128],
                   piece[:, 64 * kv:64 * kv + 64],
                   bass.AP(xlow, 128 * c, [[1024, 128], [1, 128]]),
                   start=(c == 0), stop=(c == 7))
            for ri in range(2):
                mm(psap[64 * rh:64 * rh + 64, 512 * (4 + ri):512 * (4 + ri) + 256],
                   bass.AP(xlow, 128 * c + 64 * ri, [[1024, 128], [1, 64]]),
                   piece[:, 256:512], start=(c == 0), stop=(c == 7), tp=(0, 64 * rh))
        for kv in range(NKV):
            dst = bass.AP(QK, 2048 + kv * 1024 + slot0, [[6144, 64], [512, 2], [1, 64]])
            cp("dve", dst, psap[0:64, 512 * kv:512 * kv + 128])
        for ri in range(2):
            cp("dve", Vt.ap()[64 * rh:64 * rh + 64, ri, kb, :],
               psap[64 * rh:64 * rh + 64, 512 * (4 + ri):512 * (4 + ri) + 256])

    def attn_phase(p):
        P.dma(masks.ap(), cMask[p, :, :], in_whole=True)

        for hbase in (0, 8):
            for c in range(8):
                piece = ring_load(qwT[128 * c:128 * c + 128, :], 2)
                xr = bass.AP(xlow, 128 * c, [[1024, 128], [1, 128]])
                for hh in range(8):
                    h = hbase + hh
                    mm(psap[0:64, 512 * hh:512 * hh + 128],
                       piece[:, 64 * h:64 * h + 64], xr,
                       start=(c == 0), stop=(c == 7))
            for hh in range(8):
                h = hbase + hh
                cp("dve", bass.AP(QK, (h // 2) * 256 + (h % 2) * 64,
                                  [[6144, 64], [128, 2], [1, 64]]),
                   psap[0:64, 512 * hh:512 * hh + 128])
        kv_pass(256 + 64 * p)
        for c in range(8):
            piece = ring_load(gwT[128 * c:128 * c + 128, :], 2)
            for fo in range(8):
                mm(psap[:, 512 * fo:512 * fo + 16],
                   piece[:, 128 * fo:128 * fo + 128],
                   bass.AP(xlow, 128 * c, [[1024, 128], [4, 16]]),
                   start=(c == 0), stop=(c == 7))
        gflat = g_sb.ap().rearrange("p a b -> p (a b)")
        act(gflat, bass.AP(ps, 0, [[4096, 128], [512, 8], [1, 16]]),
            AF.Exp, scale=-1.0 / 64.0)
        tsc("dve", gflat, gflat, ones_c.ap(), ALU.add)
        recip(gflat, gflat)
        kcur = 256 + 64 * (p + 1)
        nkb = (kcur + 127) // 128
        ssb = scr_ap(OFF_SSB, [[1, kcur]])
        psb = scr_ap(OFF_PSB, [[1, kcur]])
        for hp in range(8):
            kv = hp // 2
            for ri in range(2):
                sbk = psap[:, 2048:2048 + kcur]
                mm(sbk, q2_ap(hp, ri), k_ap(kv, ri, kcur), start=True, stop=True)
                tt("dve", ssb, sbk, masks.ap()[:, 0:kcur], ALU.add)
                red("dve", nmx.ap(), ssb, ALU.max, negate=True)
                tsc("dve", nmx8.ap(), nmx.ap(), 0.125, ALU.mult)
                act(psb, ssb, AF.Exp, bias=nmx8.ap(), scale=0.125, accum=ssum.ap())
                recip(rsum.ap(), ssum.ap())
                tsc("dve", psb, psb, rsum.ap(), ALU.mult)
                for kb in range(nkb):
                    w = min(128, kcur - 128 * kb)
                    transpose(psap[0:w, 2560 + 128 * kb:2560 + 128 * kb + 128],
                              scr_ap(OFF_PSB + 128 * kb, [[1, w]]))
                    cp("dve", scr_ap(OFF_PT + 128 * kb, [[1, 128]])[0:w, :],
                       psap[0:w, 2560 + 128 * kb:2560 + 128 * kb + 128])
                for j in range(2):
                    outp = psap[64 * j:64 * j + 64, 3072:3072 + 64]
                    for kb in range(nkb):
                        w = min(128, kcur - 128 * kb)
                        mm(outp, Vt.ap()[0:w, ri, kb, 64 * kv:64 * kv + 64],
                           scr_ap(OFF_PT + 128 * kb + 64 * j, [[1, 64]])[0:w, :],
                           start=(kb == 0), stop=(kb == nkb - 1), tp=(0, 64 * j))
                cp("dve", scr_ap(OFF_OCAT + 128 * hp + 64 * ri, [[1, 64]]),
                   psap[:, 3072:3072 + 64])
        xlflat = xlow.ap().rearrange("p a b c d -> p (a b c d)")
        tsc("dve", xlflat, xlflat, alph1m.ap(), ALU.mult)
        for c in range(8):
            piece = ring_load(owT[128 * c:128 * c + 128, :], 2)
            for fo in range(8):
                mm(psap[:, 512 * fo:512 * fo + 128],
                   piece[:, 128 * fo:128 * fo + 128],
                   scr_ap(OFF_OCAT + 128 * c, [[1, 128]]),
                   start=(c == 0), stop=(c == 7))
        for fo in range(8):
            ydst = bass.AP(yhat, 1056 * fo, [[8448, 128], [528, 2], [33, 16], [1, 4]])
            xsrc = bass.AP(xlow, 128 * fo, [[1024, 128], [64, 2], [4, 16], [1, 4]])
            psrc = bass.AP(ps, 512 * fo, [[4096, 128], [64, 2], [4, 16], [1, 4]])
            stt("dve", ydst, psrc, alph.ap(), xsrc, ALU.mult, ALU.add)

    def high_phase(p):
        for half in range(2):
            ch0 = 8 * half
            for ri in range(2):
                eng = "dve"
                pwT = pwrT if ri == 0 else pwiT
                base_in = 528 * ri + 33 * ch0 + 4
                for s in range(7):
                    d = 6 - s
                    for ft in range(8):
                        dwcol = dws.ap()[:, ft, ri, s:s + 1]
                        src = bass.AP(yhat, 1056 * ft + base_in, [[8448, 128], [33, 8], [1, 29 - d]])
                        dst = scr_ap(OFF_CONV + 232 * ft + d, [[29, 8], [1, 29 - d]])
                        if s == 0:
                            tsc(eng, dst, src, dwcol, ALU.mult)
                            memset(eng, scr_ap(OFF_CONV + 232 * ft, [[29, 8], [1, d]]), 0.0)
                        else:
                            stt(eng, dst, src, dwcol, dst, ALU.mult, ALU.add)
                cflat = scr_ap(OFF_CONV, [[1, 1856]])
                act(cflat, cflat, AF.Gelu)
                for c in range(8):
                    piece = ring_load(pwT[128 * c:128 * c + 128, :], 2)
                    for fo in range(8):
                        mm(psap[:, 512 * fo:512 * fo + 232],
                           piece[:, 128 * fo:128 * fo + 128],
                           scr_ap(OFF_CONV + 232 * c, [[1, 232]]),
                           start=(c == 0), stop=(c == 7))
                for fo in range(8):
                    gb = bass.AP(g_sb, 16 * fo + ch0, [[128, 128], [1, 8], [0, 29]])
                    tmp = scr_ap(OFF_TMP, [[1, 232]])
                    tt("dve", tmp, psap[:, 512 * fo:512 * fo + 232], gb, ALU.mult)
                    ysl = bass.AP(yhat, 1056 * fo + 528 * ri + 33 * ch0 + 4,
                                  [[8448, 128], [33, 8], [1, 29]])
                    tt("dve", ysl, ysl, tmp, ALU.add)
            for j in range(4):
                chq = ch0 + 2 * j
                r = chq // 2
                for fh in range(2):
                    for ri in range(2):
                        bb = 1 + ri
                        for k in range(4):
                            ft = 4 * fh + k
                            ysl = bass.AP(yhat, 1056 * ft + 528 * ri + 33 * chq,
                                          [[8448, 128], [1, 66]])
                            transpose(psap[0:66, 512 * bb + 128 * k:512 * bb + 128 * k + 128], ysl)
                        cp("dve", scr_ap(OFF_YBM + 512 * ri, [[1, 512]])[0:66, :],
                           psap[0:66, 512 * bb:512 * bb + 512])
                    outb = psap[:, 512 * 3:512 * 3 + 512]
                    mm(outb, G2rs.ap(), scr_ap(OFF_YBM, [[1, 512]])[0:66, :],
                       start=True, stop=False)
                    mm(outb, G2is.ap(), scr_ap(OFF_YBM + 512, [[1, 512]])[0:66, :],
                       start=False, stop=True)
                    xsl = xp.ap()[:, r, 512 * fh:512 * fh + 512]
                    tt("dve", xsl, xsl, outb, ALU.add)

    def ffn_phase(p):
        P.dma(Xg_d[0:4096, 1024:1026].rearrange("(g q) c -> q g c", q=128),
              zz2.ap(), out_whole=True, queue="pool")
        for r in range(8):
            q2 = r % 2
            stA = OFF_STA + 1026 * q2
            stB = OFF_STB + 1026 * q2
            rowA = scr_ap(stA, [[1, 1024]])
            layer_norm(rowA, xp.ap()[:, r, :])
            cp("dve", scr_ap(stB, [[1, 1024]]), rowA)
            for fh in range(2):
                for k in range(4):
                    ft = 4 * fh + k
                    transpose(psap[:, 512 * (4 + fh) + 128 * k:512 * (4 + fh) + 128 * k + 128],
                              scr_ap(stA + 128 * ft, [[1, 128]]))
                cp("dve", scr_ap(OFF_XN2T + 4096 * fh + 128 * r, [[1024, 4], [1, 128]]),
                   psap[:, 512 * (4 + fh):512 * (4 + fh) + 512])
            pb = psap[:, 512 * 6:512 * 6 + 16]
            for c in range(8):
                mm(pb, scr_ap(OFF_XN2T + 1024 * c + 128 * r, [[1, 128]]),
                   wrs.ap()[:, c, :], start=(c == 0), stop=(c == 7))
            red("dve", nmx.ap(), pb, ALU.max, negate=True)
            act(probs.ap(), pb, AF.Exp, bias=nmx.ap(), accum=ssum.ap())
            recip(rsum.ap(), ssum.ap())
            tsc("dve", probs.ap(), probs.ap(), rsum.ap(), ALU.mult)
            red("dve", pmx.ap(), probs.ap(), ALU.max)
            tsc("dve", msk1.ap(), probs.ap(), pmx.ap(), ALU.is_equal)
            stt("dve", pm2.ap(), msk1.ap(), nbig_c.ap(), probs.ap(), ALU.mult, ALU.add)
            red("dve", pmx.ap(), pm2.ap(), ALU.max)
            tsc("dve", msk2.ap(), pm2.ap(), pmx.ap(), ALU.is_equal)
            cp("dve", indA_s.ap()[:, r, :], msk1.ap())
            cp("dve", indB_s.ap()[:, r, :], msk2.ap())
            tt("dve", pm2.ap(), probs.ap(), msk1.ap(), ALU.mult)
            red("dve", scr_ap(stA + 1024, [[1, 1]]), pm2.ap(), ALU.add)
            tt("dve", pm2.ap(), probs.ap(), msk2.ap(), ALU.mult)
            red("dve", scr_ap(stB + 1024, [[1, 1]]), pm2.ap(), ALU.add)
            tsc("dve", scr_ap(stA + 1025, [[1, 1]]), iota_s.ap(), float(128 * r), ALU.add)
            tsc("dve", scr_ap(stB + 1025, [[1, 1]]), iota_s.ap(), float(128 * r), ALU.add)
            for bank, ind in ((6, indA_s), (7, indB_s)):
                rk = psap[:, 512 * bank:512 * bank + 16]
                mm(rk, tri_s.ap(), ind.ap()[:, r, :], start=True, stop=(r == 0))
                for rp in range(r):
                    mm(rk, ones_s.ap(), ind.ap()[:, rp, :], start=False, stop=(rp == r - 1))
            for bank, msk, st_off, idx_t, half in (
                    (6, msk1, stA, idxA, 0), (7, msk2, stB, idxB, 1)):
                rk = psap[:, 512 * bank:512 * bank + 16]
                tt("dve", pm2.ap(), rk, msk.ap(), ALU.mult)
                red("dve", sA.ap(), pm2.ap(), ALU.add)
                tt("dve", pm2.ap(), eidx_s.ap(), msk.ap(), ALU.mult)
                red("dve", sB.ap(), pm2.ap(), ALU.add)
                stt("dve", sB.ap(), sB.ap(), 256.0, sA.ap(), ALU.mult, ALU.add)
                if half:
                    tsc("dve", sB.ap(), sB.ap(), 128.0, ALU.add)
                tsc("dve", sA.ap(), sA.ap(), 127.5, ALU.is_le)
                tt("dve", sB.ap(), sB.ap(), sA.ap(), ALU.mult)
                tsc("dve", sA.ap(), sA.ap(), -1.0, ALU.mult, 1.0, ALU.add)
                tsc("dve", sA.ap(), sA.ap(), 4096.0, ALU.mult)
                tt("dve", sB.ap(), sB.ap(), sA.ap(), ALU.add)
                cp("dve", idx_t.ap(), sB.ap())
                stage_ap = scr_ap(st_off, [[1, 1026]])
                def sc_fn(h, _i=stage_ap, _x=idx_t):
                    return h.indirect_dma_start(
                        Xg_d, bass.IndirectOffsetOnAxis(_x.ap(), 0),
                        _i, None)
                P.dma(Xg_d, stage_ap, out_whole=True, queue="pool", fn=sc_fn,
                      extra_reads=(idx_t.ap(),))
        P.flush("pool", (Xg_d,))
        for ts_i in range(2):
            for hft in range(2):
                for c in range(8):
                    sg_c = ring_load(SgD[128 * c:128 * c + 128, :], 2)
                    su_c = ring_load(SuD[128 * c:128 * c + 128, :], 2)
                    rhs = scr_ap(OFF_XN2T + 1024 * c + 512 * ts_i, [[1, 512]])
                    for k in range(4):
                        ft = 4 * hft + k
                        mm(psap[:, 512 * (2 * k):512 * (2 * k) + 512],
                           sg_c[:, 128 * ft:128 * ft + 128], rhs,
                           start=(c == 0), stop=(c == 7))
                        mm(psap[:, 512 * (2 * k + 1):512 * (2 * k + 1) + 512],
                           su_c[:, 128 * ft:128 * ft + 128], rhs,
                           start=(c == 0), stop=(c == 7))
                for k in range(4):
                    ft = 4 * hft + k
                    hsl = scr_ap(OFF_HS + 512 * ft, [[1, 512]])
                    act(hsl, psap[:, 512 * (2 * k):512 * (2 * k) + 512], AF.Silu)
                    tt("dve", hsl, hsl, psap[:, 512 * (2 * k + 1):512 * (2 * k + 1) + 512], ALU.mult)
            for ft in range(8):
                sd_p = ring_load(SdD[128 * ft:128 * ft + 128, :], 2)
                for ti in range(4):
                    lhsT = scr_ap(OFF_HS + 512 * ft + 128 * ti, [[1, 128]])
                    for fh in range(2):
                        mm(psap[:, 512 * (2 * ti + fh):512 * (2 * ti + fh) + 512],
                           lhsT, sd_p[:, 512 * fh:512 * fh + 512],
                           start=(ft == 0), stop=(ft == 7))
            for ti in range(4):
                rr = 4 * ts_i + ti
                yst = scr_ap(OFF_YST, [[1, 1024]]) if rr % 2 == 0 else scr_ap(OFF_XG, [[1, 1024]])
                for fh in range(2):
                    cp("dve", scr_ap((OFF_YST if rr % 2 == 0 else OFF_XG) + 512 * fh, [[1, 512]]),
                       psap[:, 512 * (2 * ti + fh):512 * (2 * ti + fh) + 512])
                P.dma(routSH_d[128 * rr:128 * rr + 128, :], yst, out_whole=True)
        for e in range(n_experts):
            gdst = scr_ap(OFF_XG, [[1026, 2], [1, 1026]])
            P.dma(gdst, Xg_d[256 * e:256 * e + 256, :].rearrange("(b q) c -> q b c", q=128),
                  in_whole=True)
            for b in range(2):
                for fh in range(2):
                    for k in range(4):
                        ft = 4 * fh + k
                        transpose(psap[:, 512 * (4 + fh) + 128 * k:512 * (4 + fh) + 128 * k + 128],
                                  scr_ap(OFF_XG + 1026 * b + 128 * ft, [[1, 128]]))
                    cp("dve", scr_ap(OFF_XN2T + 1024 * fh + 128 * b if False else
                                     OFF_XN2T + 256 * (4 * fh) + 128 * b, [[256, 4], [1, 128]]),
                       psap[:, 512 * (4 + fh):512 * (4 + fh) + 512])
            for c in range(8):
                wg_c = ring_load(WgD[e, 128 * c:128 * c + 128, :], 1)
                wu_c = ring_load(WuD[e, 128 * c:128 * c + 128, :], 1)
                rhs = scr_ap(OFF_XN2T + 256 * c, [[1, 256]])
                for km in range(4):
                    mm(psap[:, 512 * km:512 * km + 256],
                       wg_c[:, 128 * km:128 * km + 128], rhs,
                       start=(c == 0), stop=(c == 7))
                    mm(psap[:, 512 * (4 + km):512 * (4 + km) + 256],
                       wu_c[:, 128 * km:128 * km + 128], rhs,
                       start=(c == 0), stop=(c == 7))
            for km in range(4):
                hsl = scr_ap(OFF_HE + 256 * km, [[1, 256]])
                act(hsl, psap[:, 512 * km:512 * km + 256], AF.Silu)
                tt("dve", hsl, hsl, psap[:, 512 * (4 + km):512 * (4 + km) + 256], ALU.mult)
            for km in range(4):
                wd_p = ring_load(WdD[e, 128 * km:128 * km + 128, :], 2)
                for b in range(2):
                    lhsT = scr_ap(OFF_HE + 256 * km + 128 * b, [[1, 128]])
                    for fh in range(2):
                        mm(psap[:, 512 * (2 * b + fh):512 * (2 * b + fh) + 512],
                           lhsT, wd_p[:, 512 * fh:512 * fh + 512],
                           start=(km == 0), stop=(km == 3))
            for b in range(2):
                rout_t = routA_d if b == 0 else routB_d
                cs = scr_ap(OFF_XG + 1026 * b + 1024, [[1, 1]])
                cp("dve", tokI.ap(), scr_ap(OFF_XG + 1026 * b + 1025, [[1, 1]]))
                for fh in range(2):
                    tsc("dve", scr_ap(OFF_YST + 512 * fh, [[1, 512]]),
                        psap[:, 512 * (2 * b + fh):512 * (2 * b + fh) + 512],
                        cs, ALU.mult)
                yst = scr_ap(OFF_YST, [[1, 1024]])
                def sa_fn(h, _i=yst, _x=tokI, _r=rout_t):
                    return h.indirect_dma_start(
                        _r, bass.IndirectOffsetOnAxis(_x.ap(), 0),
                        _i, None)
                P.dma(rout_t, yst, out_whole=True, queue="pool", fn=sa_fn,
                      extra_reads=(tokI.ap(),))
        P.flush("pool", (routA_d, routB_d))
        for rr in range(8):
            a_off = OFF_YST if rr % 2 == 0 else OFF_XG
            b_off = OFF_XG if rr % 2 == 0 else OFF_YST
            ya = scr_ap(a_off, [[1, 1024]])
            yb = scr_ap(b_off, [[1, 1024]])
            P.dma(ya, routSH_d[128 * rr:128 * rr + 128, :], in_whole=True)
            P.dma(yb, routA_d[128 * rr:128 * rr + 128, :], in_whole=True)
            tt("dve", ya, ya, yb, ALU.add)
            P.dma(yb, routB_d[128 * rr:128 * rr + 128, :], in_whole=True)
            tt("dve", ya, ya, yb, ALU.add)
            tt("dve", ya, ya, xp.ap()[:, rr, :], ALU.add)
            P.dma(out_d[PAIR * p + 128 * rr:PAIR * p + 128 * rr + 128, :], ya, out_whole=True)

    if prologue:
        for pp in range(NP):
            load_x(pp, x_prev)
            for r in range(8):
                ln_dft(r, low_only=True)
            kv_pass(64 * pp)
    for p in range(n_pairs):
        load_x(p, x_own)
        for r in range(8):
            ln_dft(r, low_only=False)
        tap("xn1_p%d" % p, xn1.ap()[:, 1, :])
        tap("lnmean_p%d" % p, mean_t.ap())
        tap("lnrstd_p%d" % p, rstd_t.ap())
        tap("lnstat_p%d" % p, stat.ap())
        tap("xlow_p%d" % p, xlow.ap().rearrange("p a b c d -> p (a b c d)"))
        if with_attn:
            attn_phase(p)
            tap("qk_p%d" % p, bass.AP(QK, 0, [[6144, 64], [1, 6144]]))
            tap("v_p%d" % p, Vt.ap().rearrange("p a b c -> p (a b c)"))
            tap("ocat_p%d" % p, scrap(OFF_OCAT, [8, 2, 64]).rearrange("p a b c -> p (a b c)"))
            tap("gsb_p%d" % p, g_sb.ap().rearrange("p a b -> p (a b)"))
            tap("yhlow_p%d" % p, bass.AP(yhat, 0, [[8448, 128], [1056, 8], [528, 2], [33, 16], [1, 4]]))
        if with_high:
            high_phase(p)
            tap("yhat_p%d" % p, yhat.ap().rearrange("p a b c d -> p (a b c d)"))
            tap("x1_p%d" % p, xp.ap().rearrange("p r d -> p (r d)"))
        if with_ffn:
            ffn_phase(p)

    with nc.Block() as blk:
        P.finalize(nc, blk, ctx)
    if os.environ.get("BK_DEBUG"):
        print("instructions:", nc.n_instructions(), "dmas:", P.n_dma_q,
              "nodes:", len(P.nodes))
    ctx.close()
    return nc



_BUILD_CACHE = {}

def _get_program(key=("full",)):
    if key not in _BUILD_CACHE:
        _BUILD_CACHE[key] = build_program()
    return _BUILD_CACHE[key]


def kernel(**inputs):
    ii = {k: np.asarray(v) for k, v in inputs.items()}
    X = np.ascontiguousarray(ii["L0_attn_hidden_states"], dtype=np.float32)
    F2, F2l, G2r, G2i, ident, tri_c, ones128_c, eidx_c, iota_c = host_constants()
    qwT = np.ascontiguousarray(ii["L0_attn_q_w"].T)
    kvwT = np.ascontiguousarray(np.concatenate([ii["L0_attn_k_w"].T, ii["L0_attn_v_w"].T], axis=1))
    owT = np.ascontiguousarray(ii["L0_attn_o_w"].T)
    gwT = np.ascontiguousarray(ii["L0_attn_gate_w"].T)
    pwrT = np.ascontiguousarray(ii["L0_attn_pw_r"].T)
    pwiT = np.ascontiguousarray(ii["L0_attn_pw_i"].T)
    dwp = np.zeros((128, 8, 2, 7), np.float32)
    dwp[:, :, 0, :] = ii["L0_attn_dw_r"][:, 0, :].reshape(8, 128, 7).transpose(1, 0, 2)
    dwp[:, :, 1, :] = ii["L0_attn_dw_i"][:, 0, :].reshape(8, 128, 7).transpose(1, 0, 2)
    alpha_rep = np.broadcast_to(ii["L0_attn_alpha_raw"].reshape(1, 1), (128, 1)).copy()
    Wrp = ii["L0_ffn_Wr"].reshape(8, 128, 16).transpose(1, 0, 2).copy()
    base = dict(qwT=qwT, kvwT=kvwT, owT=owT, gwT=gwT, pwrT=pwrT, pwiT=pwiT,
                dwp=dwp, alpha_in=alpha_rep,
                SgD=np.ascontiguousarray(ii["L0_ffn_Sg"]),
                SuD=np.ascontiguousarray(ii["L0_ffn_Su"]),
                SdD=np.ascontiguousarray(ii["L0_ffn_Sd"]),
                Wrp=Wrp,
                WgD=np.ascontiguousarray(ii["L0_ffn_Wg"]),
                WuD=np.ascontiguousarray(ii["L0_ffn_Wu"]),
                WdD=np.ascontiguousarray(ii["L0_ffn_Wd"]),
                cF2=F2, cF2l=F2l, cG2r=G2r, cG2i=G2i, cId=ident,
                cTri=tri_c, cOnes=ones128_c, cEidx=eidx_c, cIota=iota_c)
    mask_h = [host_masks(0), host_masks(1)]
    in_maps = []
    for core in range(8):
        b, h = core // 2, core % 2
        im = dict(base)
        im["x_own"] = np.ascontiguousarray(X[b, h * NT:(h + 1) * NT, :])
        im["x_prev"] = np.ascontiguousarray(X[b, 0:NT, :])
        im["cMask"] = mask_h[h]
        in_maps.append(im)
    nc = _get_program()
    res = run_bass_kernel_spmd(nc, in_maps, core_ids=list(range(8)))
    out = np.empty_like(X)
    for core in range(8):
        b, h = core // 2, core % 2
        out[b, h * NT:(h + 1) * NT, :] = res.results[core]["out"]
    return out
